# version 07 (final)
# baseline (speedup 1.0000x reference)
import contextlib
import numpy as np
import concourse.bass as bass
import concourse.mybir as mybir
from concourse.bass_utils import run_bass_kernel_spmd

F32 = mybir.dt.float32
BF16 = mybir.dt.bfloat16
F8 = mybir.dt.float8e4
AF = mybir.ActivationFunctionType
ALU = mybir.AluOpType

D = 2048
T = 2048
NT = 16
DT = 16
NB = 4
C = 8
BS = 256
I = 6144
IT = 48
HT = 16
RMS_EPS = float(np.float32(2.0 ** -23))
LN_EPS = 1e-5
KB = 1024


class Sem:
    def __init__(self, stack, nc, name):
        self.h = stack.enter_context(nc.semaphore(name))
        self.n = 0

    def inc(self, inst, k=1):
        inst.then_inc(self.h, k)
        self.n += k
        return self.n

    def snap(self):
        return (self.h, self.n)


def build_program(stop=None):
    nc = bass.Bass("TRN2", target_bir_lowering=False, debug=False)

    x_tok = nc.dram_tensor("x_tok", [T, D], F32, kind="ExternalInput").ap()
    wqT = nc.dram_tensor("wqT", [D, D], F32, kind="ExternalInput").ap()
    wkT = nc.dram_tensor("wkT", [D, D], F32, kind="ExternalInput").ap()
    wvT = nc.dram_tensor("wvT", [D, D], F32, kind="ExternalInput").ap()
    wgT = nc.dram_tensor("wgT", [D, D], F32, kind="ExternalInput").ap()
    wpT = nc.dram_tensor("wpT", [D, D], F32, kind="ExternalInput").ap()
    gw = nc.dram_tensor("gw", [D, I], F32, kind="ExternalInput").ap()
    uw = nc.dram_tensor("uw", [D, I], F32, kind="ExternalInput").ap()
    dw = nc.dram_tensor("dw", [I, D], F32, kind="ExternalInput").ap()
    gsc = nc.dram_tensor("gsc", [128, IT], F32, kind="ExternalInput").ap()
    usc = nc.dram_tensor("usc", [128, IT], F32, kind="ExternalInput").ap()
    dsc = nc.dram_tensor("dsc", [128, HT], F32, kind="ExternalInput").ap()
    maskin = nc.dram_tensor("maskin", [128, 2 * BS], F32, kind="ExternalInput").ap()
    identin = nc.dram_tensor("identin", [128, 128], F32, kind="ExternalInput").ap()
    flagsin = nc.dram_tensor("flagsin", [128, 2], F32, kind="ExternalInput").ap()
    out = nc.dram_tensor("out", [T, D], F32, kind="ExternalOutput").ap()

    k_sp = nc.dram_tensor("k_sp", [T, D], BF16, kind="Internal").ap()
    v_sp = nc.dram_tensor("v_sp", [T, D], BF16, kind="Internal").ap()
    qT_sp = nc.dram_tensor("qT_sp", [D, T], BF16, kind="Internal").ap()
    gT_sp = nc.dram_tensor("gT_sp", [D, T], BF16, kind="Internal").ap()
    ogT_sp = nc.dram_tensor("ogT_sp", [D, T], BF16, kind="Internal").ap()
    x2T_sp = nc.dram_tensor("x2T_sp", [D, T], BF16, kind="Internal").ap()
    fuse_sp = nc.dram_tensor("fuse_sp", [I, T], BF16, kind="Internal").ap()
    kv_snd = nc.dram_tensor("kv_snd", [D, D], BF16, kind="Internal")
    kv_rcv = nc.dram_tensor("kv_rcv", [2 * D, D], BF16, kind="Internal")

    stack = contextlib.ExitStack()

    RESERVE = 207 * KB
    got = nc.bump_sbuf(RESERVE)
    assert got is not None, "SBUF reservation failed"
    base, _sz = got

    def at(name, shape, dtype, off):
        end = off + shape[1] * mybir.dt.size(dtype)
        assert end <= RESERVE, (name, off, end)
        return nc.alloc_sbuf_tensor_at(name, shape, dtype, offset=base + off)

    R1 = at("R1", [128, 32768], BF16, 0)
    R2 = at("R2", [128, 32768], BF16, 64 * KB)
    R3 = at("R3", [128, 32768], BF16, 128 * KB)
    wcs = [at(f"wcs{i}", [128, D], BF16, 64 * KB + i * 4 * KB) for i in range(2)]
    wcs2 = [at(f"wcs2{i}", [128, D], BF16, 72 * KB + i * 4 * KB) for i in range(2)]
    kcs = [at(f"kcs{i}", [128, D], BF16, 64 * KB + i * 4 * KB) for i in range(2)]
    fuse_s = at("fuse_s", [128, IT * 1024], BF16, 0)
    dw_cs = [at(f"dw_cs{i}", [128, IT * 128], F8, 96 * KB + i * 6 * KB) for i in range(2)]
    r2T_s = at("r2T_s", [128, HT * 1024], BF16, 108 * KB)
    x2sl = [at(f"x2sl{i}", [128, 1024], BF16, 140 * KB + i * 2 * KB) for i in range(2)]
    ost = [at(f"ost{i}", [128, D], F32, 144 * KB + i * 8 * KB) for i in range(2)]
    x_t2 = [at(f"x_t2{i}", [128, D], F32, 160 * KB + i * 8 * KB) for i in range(2)]
    junk_s = at("junk_s", [128, D], BF16, 176 * KB)
    junk32a = at("junk32a", [128, D], F32, 144 * KB)
    junk32b = at("junk32b", [128, D], F32, 180 * KB)
    og2 = [at(f"og2{i}", [128, D], BF16, 180 * KB + i * 4 * KB) for i in range(2)]
    stg = [at(f"stg{i}", [128, D], BF16, 192 * KB + i * 4 * KB) for i in range(2)]
    shared8 = at("shared8", [128, DT * BS], BF16, 192 * KB)
    off = [200 * KB]

    def nxt(name, shape, dtype):
        tns = at(name, shape, dtype, off[0])
        off[0] += shape[1] * mybir.dt.size(dtype)
        off[0] = (off[0] + 31) & ~31
        return tns
    sTm_s = nxt("sTm_s", [128, 2 * BS], BF16)
    mask_s = nxt("mask_s", [128, 2 * BS], BF16)
    ident_s = nxt("ident_s", [128, 128], BF16)
    flags_s = nxt("flags_s", [128, 2], F32)
    gsc_s = nxt("gsc_s", [128, IT], F32)
    usc_s = nxt("usc_s", [128, IT], F32)
    dsc_s = nxt("dsc_s", [128, HT], F32)
    cols = nxt("cols", [128, 16], F32)
    cols2 = nxt("cols2", [128, 16], F32)
    assert off[0] <= RESERVE, off[0]

    pA = stack.enter_context(nc.psum_tensor("pA", [128, D], F32))
    pB = stack.enter_context(nc.psum_tensor("pB", [128, D], F32))

    sW = Sem(stack, nc, "sW")
    sWp = [Sem(stack, nc, "sWa"), Sem(stack, nc, "sWb")]
    sXp = [Sem(stack, nc, "sXa"), Sem(stack, nc, "sXb")]
    sFU = Sem(stack, nc, "sFU")
    sSP = Sem(stack, nc, "sSP")
    sSPp = [Sem(stack, nc, "sSPa"), Sem(stack, nc, "sSPb")]
    sOp = [Sem(stack, nc, "sOa"), Sem(stack, nc, "sOb")]
    sCC = Sem(stack, nc, "sCC")
    sT = Sem(stack, nc, "sT")
    sV = Sem(stack, nc, "sV")
    sS = Sem(stack, nc, "sS")
    block = stack.enter_context(nc.Block())

    _all_sems = [sW, sWp[0], sWp[1], sXp[0], sXp[1], sFU, sSP, sSPp[0],
                 sSPp[1], sOp[0], sOp[1], sCC, sT, sV, sS]
    _nums = sorted(s.h.num for s in _all_sems)
    assert _nums == list(range(_nums[0], _nums[-1] + 1)), _nums
    _rng = range(_nums[0], _nums[-1] + 1)

    BOOT_NOPS = 48

    @block.gpsimd
    def _(g):
        g.sem_clear(_rng)
        for _ in range(BOOT_NOPS):
            g.nop(cycle_cnt=50000)

    @block.vector
    def _(v):
        v.memset(pA[:], 0.0)
        v.memset(pB[:], 0.0)

    for _eng_name in ("scalar", "vector", "tensor", "sync"):
        def _emit(eng):
            for _ in range(BOOT_NOPS):
                eng.nop(cycle_cnt=50000)
        getattr(block, _eng_name)(_emit)

    sBOOT = Sem(stack, nc, "sBOOT")

    @block.scalar
    def _(s):
        for i_ in range(3):
            sBOOT.inc(s.dma_start(out=r2T_s[:, 0:2048],
                                  in_=k_sp[i_ * 128:(i_ + 1) * 128, 0:2048]), 16)

    @block.sync
    def _(sync):
        for i_ in range(3):
            sBOOT.inc(sync.dma_start(out=r2T_s[:, 4096:6144],
                                     in_=k_sp[i_ * 128:(i_ + 1) * 128, 0:2048]), 16)

    @block.gpsimd
    def _(g):
        g.sem_clear(range(sBOOT.h.num, sBOOT.h.num + 1))
        for i_ in range(3):
            sBOOT.inc(g.dma_start(out=r2T_s[:, 8192:10240],
                                  in_=k_sp[i_ * 128:(i_ + 1) * 128, 0:2048]), 16)
        g.wait_ge(sBOOT.h, sBOOT.n)

    def htiles(ap_):
        return ap_.rearrange("(t p) c -> p t c", p=128)

    def stiles(sb_ap, ntile):
        return sb_ap.rearrange("p (t c) -> p t c", t=ntile)

    def r1cols(i):
        return stiles(R1[:], DT)[:, :, i * 128:(i + 1) * 128]

    def wait_all(eng, snaps):
        for h, v in snaps:
            if v > 0:
                eng.wait_ge(h, v)

    def dbg_tail(snaps, src_hbm=None, src_sbuf=None):
        dbg = nc.dram_tensor("dbg", [T, D], BF16, kind="ExternalOutput").ap()

        @block.gpsimd
        def _(g):
            wait_all(g, snaps)
            if src_sbuf is not None:
                for tt in range(DT):
                    ii = g.dma_start(out=dbg[tt * 128:(tt + 1) * 128, :],
                                     in_=src_sbuf[:, tt * D:(tt + 1) * D])
                    sW.inc(ii, 16)
            else:
                ii = g.dma_start(out=dbg, in_=src_hbm)
                sW.inc(ii, 16)
            g.wait_ge(sW.h, sW.n)
        stack.close()
        return nc

    @block.gpsimd
    def _(g):
        for dst, src in ((mask_s[:], maskin), (ident_s[:], identin),
                         (flags_s[:], flagsin), (gsc_s[:], gsc),
                         (usc_s[:], usc), (dsc_s[:], dsc)):
            sW.inc(g.dma_start(out=dst, in_=src), 16)
        ii = g.dma_start(out=stiles(R2[:], DT), in_=htiles(wvT))
        sW.inc(ii, 16)
    W_WV = sW.n

    p0_s0, p0_v0, p0_t0 = sS.n, sV.n, sT.n
    p0_xv = [None] * NT

    @block.scalar
    def _(s):
        for i in range(NT):
            if i >= 2:
                s.wait_ge(sS.h, p0_s0 + (i - 2) * 3 + 3)
            ii = s.dma_start(out=x_t2[i % 2][:],
                             in_=x_tok[i * 128:(i + 1) * 128, :])
            p0_xv[i] = sXp[i % 2].inc(ii, 16)
            s.wait_ge(sV.h, p0_v0 + i * 3 + 1)
            ii = s.activation(cols[:, 1:2], cols[:, 0:1], AF.Copy,
                              scale=1.0 / D, bias=RMS_EPS)
            sS.inc(ii)
            s.wait_ge(sV.h, p0_v0 + i * 3 + 2)
            ii = s.activation(cols2[:, 0:1], cols[:, 2:3], AF.Sqrt)
            sS.inc(ii)
            s.wait_ge(sS.h, p0_s0 + i * 3 + 2)
            if i >= 2:
                s.wait_ge(sT.h, p0_t0 + (i - 2) + 1)
            ii = s.activation(stg[i % 2][:], x_t2[i % 2][:], AF.Copy,
                              scale=cols2[:, 0:1])
            sS.inc(ii)

    @block.vector
    def _(v):
        for i in range(NT):
            v.wait_ge(sXp[i % 2].h, p0_xv[i])
            if i >= 1:
                v.wait_ge(sS.h, p0_s0 + (i - 1) * 3 + 1)
            ii = v.scalar_tensor_tensor(junk32a[:], x_t2[i % 2][:], 1.0, x_t2[i % 2][:],
                                        ALU.bypass, ALU.mult, accum_out=cols[:, 0:1])
            sV.inc(ii)
            s_den = p0_s0 + i * 3 + 1
            v.wait_ge(sS.h, s_den)
            ii = v.reciprocal(cols[:, 2:3], cols[:, 1:2])
            sV.inc(ii)
            v.wait_ge(sT.h, p0_t0 + i + 1)
            ii = v.tensor_copy(r1cols(i), stiles(pA[:], DT))
            sV.inc(ii)

    @block.tensor
    def _(t):
        t.wait_ge(sW.h, W_WV)
        for i in range(NT):
            t.wait_ge(sS.h, p0_s0 + i * 3 + 3)
            if i >= 1:
                t.wait_ge(sV.h, p0_v0 + (i - 1) * 3 + 3)
            for dt2 in range(DT):
                ii = t.matmul(pA[:, dt2 * 128:(dt2 + 1) * 128],
                              stg[i % 2][:, dt2 * 128:(dt2 + 1) * 128],
                              ident_s[:], start=(dt2 % 4 == 0), stop=(dt2 % 4 == 3))
            sT.inc(ii)
    T_P0 = sT.n
    V_P0 = p0_v0 + NT * 3

    if stop == "P0":
        return dbg_tail([(sV.h, V_P0)], src_sbuf=R1[:])

    p1a_t0, p1a_s0 = sT.n, sS.n

    @block.tensor
    def _(t):
        t.wait_ge(sW.h, W_WV)
        t.wait_ge(sV.h, V_P0)
        for nt in range(NT):
            if nt >= 2:
                t.wait_ge(sS.h, p1a_s0 + (nt - 2) + 1)
            pp = [pA, pB][nt % 2]
            for dK in range(DT):
                for blk in range(NB):
                    ii = t.matmul(pp[:, blk * 512:(blk + 1) * 512],
                                  R1[:, dK * D + nt * 128: dK * D + (nt + 1) * 128],
                                  R2[:, dK * D + blk * 512: dK * D + (blk + 1) * 512],
                                  start=(dK == 0), stop=(dK == DT - 1))
            sT.inc(ii)
    T_P1A = sT.n

    @block.scalar
    def _(s):
        for nt in range(NT):
            s.wait_ge(sT.h, p1a_t0 + nt + 1)
            ii = s.activation(R3[:, nt * D:(nt + 1) * D], [pA, pB][nt % 2][:], AF.Silu)
            sS.inc(ii)
    S_P1A = sS.n

    @block.sync
    def _(sync):
        for nt in range(NT):
            sync.wait_ge(sS.h, p1a_s0 + nt + 1)
            ii = sync.dma_start(out=v_sp[nt * 128:(nt + 1) * 128, :],
                                in_=R3[:, nt * D:(nt + 1) * D])
            sSP.inc(ii, 16)
    SPV = [sSP.snap()]

    if stop == "P1a":
        return dbg_tail(SPV, src_hbm=v_sp)

    @block.gpsimd
    def _(g):
        g.wait_ge(sT.h, T_P1A)
        ii = g.dma_start(out=stiles(R2[:], DT), in_=htiles(wkT))
        sW.inc(ii, 16)
    W_WK = sW.n

    p1b_t0, p1b_s0 = sT.n, sS.n
    p1b_spv = [None] * NT

    @block.tensor
    def _(t):
        t.wait_ge(sW.h, W_WK)
        for nt in range(NT):
            if nt >= 2:
                t.wait_ge(sS.h, p1b_s0 + (nt - 2) + 1)
            pp = [pA, pB][nt % 2]
            for dK in range(DT):
                for blk in range(NB):
                    ii = t.matmul(pp[:, blk * 512:(blk + 1) * 512],
                                  R1[:, dK * D + nt * 128: dK * D + (nt + 1) * 128],
                                  R2[:, dK * D + blk * 512: dK * D + (blk + 1) * 512],
                                  start=(dK == 0), stop=(dK == DT - 1))
            sT.inc(ii)
    T_P1B = sT.n

    @block.sync
    def _(sync):
        for nt in range(NT):
            sync.wait_ge(sS.h, p1b_s0 + nt + 1)
            ii = sync.dma_start(out=k_sp[nt * 128:(nt + 1) * 128, :],
                                in_=stg[nt % 2][:])
            p1b_spv[nt] = sSPp[nt % 2].inc(ii, 16)

    @block.scalar
    def _(s):
        for nt in range(NT):
            s.wait_ge(sT.h, p1b_t0 + nt + 1)
            if nt >= 2:
                s.wait_ge(sSPp[nt % 2].h, p1b_spv[nt - 2])
            ii = s.activation(stg[nt % 2][:], [pA, pB][nt % 2][:], AF.Silu)
            sS.inc(ii)
    SPK = [sSPp[0].snap(), sSPp[1].snap()]

    if stop == "P1b":
        return dbg_tail(SPK, src_hbm=k_sp)

    p2_t0, p2_v0 = sT.n, sV.n
    p2_xv = [None] * DT
    p2_spv = [None] * DT

    @block.scalar
    def _(s):
        for dt2 in range(DT):
            if dt2 < 2:
                wait_all(s, SPK)
            else:
                s.wait_ge(sT.h, p2_t0 + (dt2 - 2) + 1)
            ii = s.dma_start(out=stiles(kcs[dt2 % 2][:], NT),
                             in_=htiles(k_sp[:, dt2 * 128:(dt2 + 1) * 128]))
            p2_xv[dt2] = sXp[dt2 % 2].inc(ii, 16)

    @block.tensor
    def _(t):
        for dt2 in range(DT):
            t.wait_ge(sXp[dt2 % 2].h, p2_xv[dt2])
            if dt2 >= 2:
                t.wait_ge(sV.h, p2_v0 + (dt2 - 2) + 1)
            pp = [pA, pB][dt2 % 2]
            for jK in range(NT):
                for blk in range(NB):
                    ii = t.matmul(pp[:, blk * 512:(blk + 1) * 512],
                                  kcs[dt2 % 2][:, jK * 128:(jK + 1) * 128],
                                  R3[:, jK * D + blk * 512: jK * D + (blk + 1) * 512],
                                  start=(jK == 0), stop=(jK == NT - 1))
            sT.inc(ii)
    T_P2 = sT.n

    @block.sync
    def _(sync):
        for dt2 in range(DT):
            sync.wait_ge(sV.h, p2_v0 + dt2 + 1)
            ii = sync.dma_start(out=kv_snd.ap()[dt2 * 128:(dt2 + 1) * 128, :],
                                in_=stg[dt2 % 2][:])
            p2_spv[dt2] = sSPp[dt2 % 2].inc(ii, 16)

    @block.vector
    def _(v):
        for dt2 in range(DT):
            v.wait_ge(sT.h, p2_t0 + dt2 + 1)
            if dt2 < 2:
                wait_all(v, SPK)
            else:
                v.wait_ge(sSPp[dt2 % 2].h, p2_spv[dt2 - 2])
            ii = v.tensor_copy(stg[dt2 % 2][:], [pA, pB][dt2 % 2][:])
            sV.inc(ii)
    V_P2 = sV.n
    SPSND = [sSPp[0].snap(), sSPp[1].snap()]

    @block.gpsimd
    def _(g):
        wait_all(g, SPSND)
        ii = g.collective_compute(
            "AllGather", ALU.bypass,
            replica_groups=[[0, 1], [2, 3], [4, 5], [6, 7]],
            ins=[kv_snd.ap().opt()],
            outs=[kv_rcv.ap().opt()],
        )
        sCC.inc(ii, 1)
    CC_DONE = sCC.n

    if stop == "P2":
        return dbg_tail([(sCC.h, CC_DONE)], src_hbm=kv_rcv.ap()[0:D, :])

    def proj_feature(wT_h, spill, func, wbufs, entry_t):
        f_t0, f_s0 = sT.n, sS.n
        wv_rec = [None] * NT
        sp_rec = [None] * NT

        @block.gpsimd
        def _(g):
            for ot in range(NT):
                if ot < 2:
                    g.wait_ge(sT.h, entry_t)
                else:
                    g.wait_ge(sT.h, f_t0 + (ot - 2) + 1)
                ii = g.dma_start(out=stiles(wbufs[ot % 2][:], DT),
                                 in_=htiles(wT_h[:, ot * 128:(ot + 1) * 128]))
                wv_rec[ot] = sWp[ot % 2].inc(ii, 16)

        @block.tensor
        def _(t):
            for ot in range(NT):
                t.wait_ge(sWp[ot % 2].h, wv_rec[ot])
                if ot >= 2:
                    t.wait_ge(sS.h, f_s0 + (ot - 2) + 1)
                pp = [pA, pB][ot % 2]
                for dK in range(DT):
                    for blk in range(NB):
                        ii = t.matmul(pp[:, blk * 512:(blk + 1) * 512],
                                      wbufs[ot % 2][:, dK * 128:(dK + 1) * 128],
                                      R1[:, dK * D + blk * 512: dK * D + (blk + 1) * 512],
                                      start=(dK == 0), stop=(dK == DT - 1))
                sT.inc(ii)

        @block.sync
        def _(sync):
            for ot in range(NT):
                sync.wait_ge(sS.h, f_s0 + ot + 1)
                ii = sync.dma_start(out=spill[ot * 128:(ot + 1) * 128, :],
                                    in_=stg[ot % 2][:])
                sp_rec[ot] = sSPp[ot % 2].inc(ii, 16)

        @block.scalar
        def _(s):
            for ot in range(NT):
                s.wait_ge(sT.h, f_t0 + ot + 1)
                if ot >= 2:
                    s.wait_ge(sSPp[ot % 2].h, sp_rec[ot - 2])
                ii = s.activation(stg[ot % 2][:], [pA, pB][ot % 2][:], func)
                sS.inc(ii)

    proj_feature(wqT, qT_sp, AF.Silu, wcs, T_P2)
    SPQ = [sSPp[0].snap(), sSPp[1].snap()]
    if stop == "P3q":
        return dbg_tail(SPQ, src_hbm=qT_sp)
    proj_feature(wgT, gT_sp, AF.Sigmoid, wcs2, T_P2)
    SPG = [sSPp[0].snap(), sSPp[1].snap()]
    T_P3 = sT.n
    S_P3 = sS.n
    if stop == "P3g":
        return dbg_tail(SPG, src_hbm=gT_sp)

    @block.gpsimd
    def _(g):
        g.wait_ge(sT.h, T_P3)
        ii = g.dma_start(out=stiles(R2[:], DT), in_=htiles(wpT))
        sW.inc(ii, 16)
    W_WP = sW.n

    @block.gpsimd
    def _(g):
        g.wait_ge(sCC.h, CC_DONE)
        g.wait_ge(sT.h, T_P2)
        ii = g.dma_start(out=stiles(R3[:], DT), in_=htiles(kv_rcv.ap()[0:D, :]))
        sW.inc(ii, 16)
    W_RB0 = sW.n

    @block.vector
    def _(v):
        v.wait_ge(sW.h, W_RB0)
        v.wait_ge(sT.h, T_P3)
        ii = v.tensor_scalar(R1[:], R3[:], flags_s[:, 0:1], None, ALU.mult)
        sV.inc(ii)
    V_ASM0 = sV.n

    @block.gpsimd
    def _(g):
        g.wait_ge(sV.h, V_ASM0)
        ii = g.dma_start(out=stiles(R3[:], DT), in_=htiles(kv_rcv.ap()[D:2 * D, :]))
        sW.inc(ii, 16)
    W_RB1 = sW.n

    @block.vector
    def _(v):
        v.wait_ge(sW.h, W_RB1)
        ii = v.scalar_tensor_tensor(R1[:], R3[:], flags_s[:, 1:2], R1[:],
                                    ALU.mult, ALU.add)
        sV.inc(ii)
    V_ASM = sV.n

    if stop == "ASM":
        return dbg_tail([(sV.h, V_ASM)], src_sbuf=R1[:])

    qTc = [R3[:, 0:4096], R3[:, 4096:8192]]
    gTc = [R3[:, 8192:12288], R3[:, 12288:16384]]
    kc = [R3[:, 16384:20480], R3[:, 20480:24576]]
    vc = [R3[:, 24576:28672], R3[:, 28672:32768]]
    TM, VM = 21, 21
    r_t0, r_v0 = sT.n, sV.n
    rec_xv = [None] * C
    rec_ogv = [None] * C

    @block.scalar
    def _(s):
        for c in range(C):
            if c < 2:
                wait_all(s, SPG)
                s.wait_ge(sV.h, V_ASM)
            else:
                s.wait_ge(sT.h, r_t0 + (c - 1) * TM)
                s.wait_ge(sV.h, r_v0 + (c - 2) * VM + 5)
            b = c % 2
            s.dma_start(out=stiles(qTc[b], DT),
                        in_=htiles(qT_sp[:, c * BS:(c + 1) * BS])).then_inc(sXp[b].h, 16)
            s.dma_start(out=stiles(gTc[b], DT),
                        in_=htiles(gT_sp[:, c * BS:(c + 1) * BS])).then_inc(sXp[b].h, 16)
            s.dma_start(out=stiles(kc[b], 2),
                        in_=htiles(k_sp[c * BS:(c + 1) * BS, :])).then_inc(sXp[b].h, 16)
            ii = s.dma_start(out=stiles(vc[b], 2),
                             in_=htiles(v_sp[c * BS:(c + 1) * BS, :]))
            sXp[b].n += 3 * 16
            rec_xv[c] = sXp[b].inc(ii, 16)

    @block.tensor
    def _(t):
        t.wait_ge(sV.h, V_ASM)
        t.wait_ge(sS.h, S_P3)
        for c in range(C):
            b = c % 2
            tb = r_t0 + c * TM
            vb = r_v0 + c * VM
            t.wait_ge(sXp[b].h, rec_xv[c])
            for r in range(2):
                if c > 0 and r == 0:
                    t.wait_ge(sV.h, vb)
                if r == 1:
                    t.wait_ge(sV.h, vb + 1)
                for dt2 in range(r * 8, r * 8 + 8):
                    for jt in range(2):
                        ii = t.matmul(
                            pA[:, (dt2 % 8) * 256 + jt * 128:(dt2 % 8) * 256 + (jt + 1) * 128],
                            kc[b][:, jt * D + dt2 * 128: jt * D + (dt2 + 1) * 128],
                            ident_s[:], start=(dt2 % 2 == 0 and jt == 0),
                            stop=(dt2 % 2 == 1 and jt == 1))
                sT.inc(ii)
            t.wait_ge(sV.h, vb + 2)
            for jt in range(2):
                for dK in range(DT):
                    ii = t.matmul(pB[:, jt * BS:(jt + 1) * BS],
                                  shared8[:, dK * BS + jt * 128: dK * BS + (jt + 1) * 128],
                                  qTc[b][:, dK * BS:(dK + 1) * BS],
                                  start=(jt == 0 and dK == 0),
                                  stop=(jt == 1 and dK == DT - 1))
            sT.inc(ii)
            for eh in range(2):
                t.wait_ge(sV.h, vb + 3 + eh)
                for et in range(8):
                    e8 = eh * 8 + et
                    for dK in range(DT):
                        t.matmul(pA[:, et * BS:(et + 1) * BS],
                                 R1[:, dK * D + e8 * 128: dK * D + (e8 + 1) * 128],
                                 qTc[b][:, dK * BS:(dK + 1) * BS],
                                 start=(dK == 0 and et % 2 == 0), stop=False)
                    for jt in range(2):
                        ii = t.matmul(pA[:, et * BS:(et + 1) * BS],
                                      vc[b][:, jt * D + e8 * 128: jt * D + (e8 + 1) * 128],
                                      sTm_s[:, jt * BS:(jt + 1) * BS],
                                      start=False, stop=(jt == 1 and et % 2 == 1))
                sT.inc(ii)
            for dt2 in range(DT):
                if dt2 == 1:
                    t.wait_ge(sV.h, vb + 5)
                elif dt2 >= 2:
                    t.wait_ge(sV.h, vb + 5 + (dt2 - 2) + 1)
                pp = [pB, pA][dt2 % 2]
                for jt in range(2):
                    for blk in range(NB):
                        ii = t.matmul(pp[:, blk * 512:(blk + 1) * 512],
                                      kc[b][:, jt * D + dt2 * 128: jt * D + (dt2 + 1) * 128],
                                      vc[b][:, jt * D + blk * 512: jt * D + (blk + 1) * 512],
                                      start=(jt == 0), stop=(jt == 1))
                sT.inc(ii)
    T_REC = sT.n
    assert T_REC == r_t0 + C * TM

    @block.sync
    def _(sync):
        for c in range(C):
            vb = r_v0 + c * VM
            sync.wait_ge(sV.h, vb + 5)
            ii = sync.dma_start(out=htiles(ogT_sp[:, c * BS:(c + 1) * BS]),
                                in_=stiles(shared8[:], DT))
            rec_ogv[c] = sSP.inc(ii, 16)
    SPOG = [sSP.snap()]

    @block.vector
    def _(v):
        for c in range(C):
            b = c % 2
            tb = r_t0 + c * TM
            vb = r_v0 + c * VM
            for r in range(2):
                v.wait_ge(sT.h, tb + r + 1)
                if r == 0 and c >= 1:
                    v.wait_ge(sSP.h, rec_ogv[c - 1])
                ii = v.tensor_copy(stiles(shared8[:, r * 2048:(r + 1) * 2048], 8),
                                   stiles(pA[:], 8))
                sV.inc(ii)
            v.wait_ge(sT.h, tb + 3)
            ii = v.tensor_tensor(sTm_s[:], pB[:, 0:2 * BS], mask_s[:], ALU.mult)
            sV.inc(ii)
            for eh in range(2):
                v.wait_ge(sT.h, tb + 4 + eh)
                ii = v.tensor_tensor(shared8[:, eh * 2048:(eh + 1) * 2048],
                                     pA[:, 0:2048],
                                     gTc[b][:, eh * 2048:(eh + 1) * 2048],
                                     ALU.mult)
                sV.inc(ii)
            for dt2 in range(DT):
                v.wait_ge(sT.h, tb + 6 + dt2)
                pp = [pB, pA][dt2 % 2]
                ii = v.tensor_tensor(R1[:, dt2 * D:(dt2 + 1) * D],
                                     pp[:], R1[:, dt2 * D:(dt2 + 1) * D], ALU.add)
                sV.inc(ii)
    V_REC = sV.n
    assert V_REC == r_v0 + C * VM


    if stop == "REC":
        return dbg_tail(SPOG, src_hbm=ogT_sp)

    p4_t0, p4_v0, p4_s0 = sT.n, sV.n, sS.n
    p4_xv = [None] * NT

    @block.scalar
    def _(s):
        for i in range(NT):
            if i < 2:
                wait_all(s, SPOG)
                s.wait_ge(sT.h, T_REC)
            else:
                s.wait_ge(sT.h, p4_t0 + (i - 2) * 2 + 1)
                s.wait_ge(sS.h, p4_s0 + (i - 2) * 3 + 2)
            b = i % 2
            s.dma_start(out=stiles(og2[b][:], DT),
                        in_=htiles(ogT_sp[:, i * 128:(i + 1) * 128])).then_inc(sXp[b].h, 16)
            ii = s.dma_start(out=x_t2[b][:],
                             in_=x_tok[i * 128:(i + 1) * 128, :])
            sXp[b].n += 16
            p4_xv[i] = sXp[b].inc(ii, 16)
            s.wait_ge(sV.h, p4_v0 + i * 6 + 1)
            ii = s.activation(cols[:, 6:7], cols[:, 4:5], AF.Copy, scale=1.0 / D)
            sS.inc(ii)
            s.wait_ge(sV.h, p4_v0 + i * 6 + 2)
            ii = s.activation(x_t2[b][:], junk32a[:], AF.Square, accum_out=cols[:, 5:6])
            sS.inc(ii)
            s.wait_ge(sV.h, p4_v0 + i * 6 + 4)
            ii = s.activation(cols2[:, 1:2], cols[:, 8:9], AF.Sqrt)
            sS.inc(ii)

    @block.tensor
    def _(t):
        t.wait_ge(sW.h, W_WP)
        t.wait_ge(sV.h, V_REC)
        for i in range(NT):
            t.wait_ge(sXp[i % 2].h, p4_xv[i])
            if i >= 2:
                t.wait_ge(sV.h, p4_v0 + (i - 2) * 6 + 6)
            pp = [pA, pB][i % 2]
            for dK in range(DT):
                for blk in range(NB):
                    ii = t.matmul(pp[:, blk * 512:(blk + 1) * 512],
                                  og2[i % 2][:, dK * 128:(dK + 1) * 128],
                                  R2[:, dK * D + blk * 512: dK * D + (blk + 1) * 512],
                                  start=(dK == 0), stop=(dK == DT - 1))
            sT.inc(ii)
            t.wait_ge(sV.h, p4_v0 + i * 6 + 5)
            for dt2 in range(DT):
                ii = t.matmul(pp[:, dt2 * 128:(dt2 + 1) * 128],
                              stg[i % 2][:, dt2 * 128:(dt2 + 1) * 128],
                              ident_s[:], start=(dt2 % 4 == 0), stop=(dt2 % 4 == 3))
            sT.inc(ii)
    T_P4 = sT.n

    pp0_ = [pA, pB]

    @block.vector
    def _(v):
        for i in range(NT):
            tb = p4_t0 + i * 2
            b = i % 2
            v.wait_ge(sT.h, tb + 1)
            if i >= 1:
                v.wait_ge(sS.h, p4_s0 + (i - 1) * 3 + 1)
            ii = v.scalar_tensor_tensor(x_t2[b][:], x_t2[b][:], 1.0, pp0_[b][:],
                                        ALU.bypass, ALU.add, accum_out=cols[:, 4:5])
            sV.inc(ii)
            v.wait_ge(sS.h, p4_s0 + i * 3 + 1)
            ii = v.tensor_scalar(junk32a[:], x_t2[b][:], cols[:, 6:7], None,
                                 ALU.subtract)
            sV.inc(ii)
            v.wait_ge(sS.h, p4_s0 + i * 3 + 2)
            ii = v.tensor_scalar(cols[:, 7:8], cols[:, 5:6], 1.0 / D, LN_EPS,
                                 ALU.mult, ALU.add)
            sV.inc(ii)
            v.wait_ge(sV.h, p4_v0 + i * 6 + 3)
            ii = v.reciprocal(cols[:, 8:9], cols[:, 7:8])
            sV.inc(ii)
            v.wait_ge(sS.h, p4_s0 + i * 3 + 3)
            if i < 2:
                wait_all(v, SPG)
            else:
                v.wait_ge(sT.h, p4_t0 + (i - 2) * 2 + 2)
            ii = v.tensor_scalar(stg[b][:], junk32a[:], cols2[:, 1:2], None,
                                 ALU.mult)
            sV.inc(ii)
            v.wait_ge(sT.h, tb + 2)
            ii = v.tensor_copy(r1cols(i), stiles([pA, pB][b][:], DT))
            sV.inc(ii)
    V_P4 = sV.n

    @block.sync
    def _(sync):
        for i in range(NT):
            sync.wait_ge(sV.h, p4_v0 + i * 6 + 6)
            ii = sync.dma_start(out=htiles(x2T_sp[:, i * 128:(i + 1) * 128]),
                                in_=r1cols(i))
            sSP.inc(ii, 16)
    SPX2T = [sSP.snap()]

    if stop == "P4":
        return dbg_tail(SPX2T, src_hbm=x2T_sp)

    gw_cs = [dw_cs[0][:, 0:2048], dw_cs[0][:, 2048:4096]]
    uw_cs = [dw_cs[1][:, 0:2048], dw_cs[1][:, 2048:4096]]
    p5_t0, p5_v0, p5_s0 = sT.n, sV.n, sS.n
    p5_wv = [None] * IT
    p5_spv = [None] * IT

    @block.gpsimd
    def _(g):
        for it in range(IT):
            if it < 2:
                g.wait_ge(sV.h, V_P4)
            else:
                g.wait_ge(sT.h, p5_t0 + (it - 2) * 2 + 2)
            b = it % 2
            g.dma_start(out=stiles(gw_cs[b], DT),
                        in_=htiles(gw[:, it * 128:(it + 1) * 128])).then_inc(sWp[b].h, 16)
            ii = g.dma_start(out=stiles(uw_cs[b], DT),
                             in_=htiles(uw[:, it * 128:(it + 1) * 128]))
            sWp[b].n += 16
            p5_wv[it] = sWp[b].inc(ii, 16)

    @block.tensor
    def _(t):
        t.wait_ge(sV.h, V_P4)
        for it in range(IT):
            t.wait_ge(sWp[it % 2].h, p5_wv[it])
            b = it % 2
            if it >= 1:
                t.wait_ge(sS.h, p5_s0 + (it - 1) + 1)
            for dK in range(DT):
                for blk in range(NB):
                    ii = t.matmul(pA[:, blk * 512:(blk + 1) * 512],
                                  gw_cs[b][:, dK * 128:(dK + 1) * 128],
                                  R1[:, dK * D + blk * 512: dK * D + (blk + 1) * 512],
                                  start=(dK == 0), stop=(dK == DT - 1))
            sT.inc(ii)
            if it >= 1:
                t.wait_ge(sV.h, p5_v0 + (it - 1) + 1)
            for dK in range(DT):
                for blk in range(NB):
                    ii = t.matmul(pB[:, blk * 512:(blk + 1) * 512],
                                  uw_cs[b][:, dK * 128:(dK + 1) * 128],
                                  R1[:, dK * D + blk * 512: dK * D + (blk + 1) * 512],
                                  start=(dK == 0), stop=(dK == DT - 1))
            sT.inc(ii)
    T_P5 = sT.n

    @block.scalar
    def _(s):
        for it in range(IT):
            s.wait_ge(sT.h, p5_t0 + it * 2 + 1)
            if it >= 2:
                s.wait_ge(sV.h, p5_v0 + (it - 2) + 1)
            ii = s.activation(stg[it % 2][:], pA[:], AF.Silu,
                              scale=gsc_s[:, it:it + 1])
            sS.inc(ii)
    S_P5 = sS.n

    @block.sync
    def _(sync):
        for it in range(IT):
            sync.wait_ge(sV.h, p5_v0 + it + 1)
            ii = sync.dma_start(out=fuse_sp[it * 128:(it + 1) * 128, :],
                                in_=wcs[it % 2][:])
            p5_spv[it] = sSPp[it % 2].inc(ii, 16)

    @block.vector
    def _(v):
        for it in range(IT):
            v.wait_ge(sT.h, p5_t0 + it * 2 + 2)
            v.wait_ge(sS.h, p5_s0 + it + 1)
            if it >= 2:
                v.wait_ge(sSPp[it % 2].h, p5_spv[it - 2])
            ii = v.scalar_tensor_tensor(wcs[it % 2][:], pB[:], usc_s[:, it:it + 1],
                                        stg[it % 2][:], ALU.mult, ALU.mult)
            sV.inc(ii)
    V_P5 = sV.n
    SPFUSE = [sSPp[0].snap(), sSPp[1].snap()]

    if stop == "P5":
        return dbg_tail(SPFUSE, src_hbm=fuse_sp[0:T, :])

    p6_t0, p6_v0 = sT.n, sV.n
    TM6 = 24
    VM6 = 56
    p6_fuv = [None] * 2
    p6_x2v = [None] * 32
    p6_dwv = [None] * 32
    p6_s0 = sS.n

    def evac_ms(hg):
        return p6_v0 + (hg // HT) * VM6 + (hg % HT) + 1

    @block.sync
    def _(sync):
        for nh in range(2):
            if nh == 0:
                sync.wait_ge(sT.h, T_P5)
                wait_all(sync, SPFUSE)
            else:
                sync.wait_ge(sT.h, p6_t0 + 16)
            for fk in range(6):
                ii = sync.dma_start(
                    out=stiles(fuse_s[:], IT)[:, fk * 8:(fk + 1) * 8, :],
                    in_=htiles(fuse_sp[:, nh * 1024:(nh + 1) * 1024])[:, fk * 8:(fk + 1) * 8, :])
                p6_fuv[nh] = sFU.inc(ii, 16)
            for ht in range(HT):
                hg = nh * HT + ht
                if hg == 0:
                    wait_all(sync, SPX2T)
                if hg >= 2:
                    sync.wait_ge(sV.h, evac_ms(hg - 2))
                ii = sync.dma_start(out=x2sl[ht % 2][:],
                                    in_=x2T_sp[ht * 128:(ht + 1) * 128,
                                               nh * 1024:(nh + 1) * 1024])
                p6_x2v[hg] = sXp[hg % 2].inc(ii, 16)
            for ntl in range(8):
                g6 = nh * 8 + ntl
                vb6 = p6_v0 + nh * VM6 + HT + ntl * 5
                sync.wait_ge(sV.h, vb6 + 5)
                ii = sync.dma_start(out=out[g6 * 128:(g6 + 1) * 128, :],
                                    in_=ost[ntl % 2][:])
                sOp[g6 % 2].inc(ii, 16)
        sync.wait_ge(sOp[0].h, sOp[0].n)
        sync.wait_ge(sOp[1].h, sOp[1].n)

    @block.gpsimd
    def _(g):
        for nh in range(2):
            for ht in range(HT):
                hg = nh * HT + ht
                if hg < 2:
                    g.wait_ge(sT.h, T_P5)
                else:
                    g.wait_ge(sV.h, evac_ms(hg - 2))
                for fk in range(4):
                    ii = g.dma_start(
                        out=stiles(dw_cs[ht % 2][:], IT)[:, fk * 12:(fk + 1) * 12, :],
                        in_=htiles(dw[:, ht * 128:(ht + 1) * 128])[:, fk * 12:(fk + 1) * 12, :])
                    p6_dwv[hg] = sWp[hg % 2].inc(ii, 16)

    @block.tensor
    def _(t):
        for nh in range(2):
            tb = p6_t0 + nh * TM6
            t.wait_ge(sFU.h, p6_fuv[nh])
            if nh == 0:
                t.wait_ge(sV.h, V_P5)
                t.wait_ge(sS.h, S_P5)
            for ht in range(HT):
                hg = nh * HT + ht
                t.wait_ge(sWp[hg % 2].h, p6_dwv[hg])
                if hg >= 2:
                    t.wait_ge(sV.h, evac_ms(hg - 2))
                po = (ht % 2) * 1024
                for iK in range(IT):
                    for sj in range(2):
                        ii = t.matmul(pA[:, po + sj * 512: po + (sj + 1) * 512],
                                      dw_cs[ht % 2][:, iK * 128:(iK + 1) * 128],
                                      fuse_s[:, iK * 1024 + sj * 512: iK * 1024 + (sj + 1) * 512],
                                      start=(iK == 0), stop=(iK == IT - 1))
                sT.inc(ii)
            for ntl in range(8):
                g6 = nh * 8 + ntl
                vb = p6_v0 + nh * VM6
                t.wait_ge(sV.h, vb + HT)
                if g6 >= 1:
                    t.wait_ge(sV.h, p6_v0 + ((g6 - 1) // 8) * VM6 + HT + ((g6 - 1) % 8) * 5 + 1)
                for ht2 in range(HT):
                    ii = t.matmul(pB[:, ht2 * 128:(ht2 + 1) * 128],
                                  r2T_s[:, ht2 * 1024 + ntl * 128: ht2 * 1024 + (ntl + 1) * 128],
                                  ident_s[:], start=(ht2 % 4 == 0), stop=(ht2 % 4 == 3))
                sT.inc(ii)

    @block.vector
    def _(v):
        for nh in range(2):
            tb = p6_t0 + nh * TM6
            for ht in range(HT):
                hg = nh * HT + ht
                v.wait_ge(sT.h, tb + ht + 1)
                v.wait_ge(sXp[hg % 2].h, p6_x2v[hg])
                po = (ht % 2) * 1024
                ii = v.scalar_tensor_tensor(r2T_s[:, ht * 1024:(ht + 1) * 1024],
                                            pA[:, po:po + 1024], dsc_s[:, ht:ht + 1],
                                            x2sl[ht % 2][:], ALU.mult, ALU.add)
                sV.inc(ii)
            for ntl in range(8):
                g6 = nh * 8 + ntl
                b = ntl % 2
                tb6 = p6_t0 + nh * TM6
                v.wait_ge(sT.h, tb6 + HT + ntl + 1)
                if g6 >= 1:
                    v.wait_ge(sS.h, p6_s0 + (g6 - 1) * 3 + 1)
                ii = v.tensor_scalar(x_t2[b][:], pB[:], 1.0, 0.0, ALU.mult,
                                     ALU.add, accum_out=cols[:, 4:5])
                sV.inc(ii)
                v.wait_ge(sS.h, p6_s0 + g6 * 3 + 1)
                ii = v.tensor_scalar(junk32b[:], x_t2[b][:], cols[:, 6:7], None,
                                     ALU.subtract)
                sV.inc(ii)
                vb6 = p6_v0 + nh * VM6 + HT + ntl * 5
                v.wait_ge(sS.h, p6_s0 + g6 * 3 + 2)
                ii = v.tensor_scalar(cols[:, 7:8], cols[:, 5:6], 1.0 / D, LN_EPS,
                                     ALU.mult, ALU.add)
                sV.inc(ii)
                v.wait_ge(sV.h, vb6 + 3)
                ii = v.reciprocal(cols[:, 8:9], cols[:, 7:8])
                sV.inc(ii)
                v.wait_ge(sS.h, p6_s0 + g6 * 3 + 3)
                if g6 >= 2:
                    v.wait_ge(sOp[g6 % 2].h, ((g6 - 2) // 2 + 1) * 16)
                ii = v.tensor_scalar(ost[b][:], junk32b[:], cols2[:, 1:2], None,
                                     ALU.mult)
                sV.inc(ii)

    @block.scalar
    def _(s):
        for nh in range(2):
            for ntl in range(8):
                g6 = nh * 8 + ntl
                b = ntl % 2
                vb6 = p6_v0 + nh * VM6 + HT + ntl * 5
                s.wait_ge(sV.h, vb6 + 1)
                ii = s.activation(cols[:, 6:7], cols[:, 4:5], AF.Copy, scale=1.0 / D)
                sS.inc(ii)
                s.wait_ge(sV.h, vb6 + 2)
                ii = s.activation(x_t2[b][:], junk32b[:], AF.Square,
                                  accum_out=cols[:, 5:6])
                sS.inc(ii)
                s.wait_ge(sV.h, vb6 + 4)
                ii = s.activation(cols2[:, 1:2], cols[:, 8:9], AF.Sqrt)
                sS.inc(ii)

    stack.close()
    return nc


_NC_CACHE = []


def _get_nc():
    if not _NC_CACHE:
        _NC_CACHE.append(build_program())
    return _NC_CACHE[0]


def kernel(L0_attn_x, L0_attn_Wq, L0_attn_Wk, L0_attn_Wv, L0_attn_Wg, L0_attn_Wp,
           L0_ffn_gate_w, L0_ffn_up_w, L0_ffn_down_w,
           L0_ffn_gate_scale, L0_ffn_up_scale, L0_ffn_down_scale):
    f32 = np.float32
    x = np.ascontiguousarray(np.asarray(L0_attn_x, dtype=f32))
    wq = np.ascontiguousarray(np.asarray(L0_attn_Wq, dtype=f32).T)
    wk = np.ascontiguousarray(np.asarray(L0_attn_Wk, dtype=f32).T)
    wv = np.ascontiguousarray(np.asarray(L0_attn_Wv, dtype=f32).T)
    wg = np.ascontiguousarray(np.asarray(L0_attn_Wg, dtype=f32).T)
    wp = np.ascontiguousarray(np.asarray(L0_attn_Wp, dtype=f32).T)
    gwa = np.ascontiguousarray(np.asarray(L0_ffn_gate_w, dtype=f32))
    uwa = np.ascontiguousarray(np.asarray(L0_ffn_up_w, dtype=f32))
    dwa = np.ascontiguousarray(np.asarray(L0_ffn_down_w, dtype=f32))
    gsc = np.ascontiguousarray(np.asarray(L0_ffn_gate_scale, dtype=f32).reshape(IT, 128).T)
    usc = np.ascontiguousarray(np.asarray(L0_ffn_up_scale, dtype=f32).reshape(IT, 128).T)
    dsc = np.ascontiguousarray(np.asarray(L0_ffn_down_scale, dtype=f32).reshape(HT, 128).T)
    M = np.triu(np.ones((BS, BS), f32))
    mstage = np.ascontiguousarray(M.reshape(2, 128, BS).transpose(1, 0, 2).reshape(128, 2 * BS))
    ident = np.eye(128, dtype=f32)

    nc = _get_nc()
    in_maps = []
    for c in range(8):
        b, h = c // 2, c % 2
        flags = np.zeros((128, 2), f32)
        if h == 1:
            flags[:, 0] = 1.0
        in_maps.append({
            "x_tok": np.ascontiguousarray(x[b, h * T:(h + 1) * T, :]),
            "wqT": wq, "wkT": wk, "wvT": wv, "wgT": wg, "wpT": wp,
            "gw": gwa, "uw": uwa, "dw": dwa,
            "gsc": gsc, "usc": usc, "dsc": dsc,
            "maskin": mstage, "identin": ident, "flagsin": flags,
        })
    res = run_bass_kernel_spmd(nc, in_maps, core_ids=list(range(8)))
    outp = np.empty((4, 4096, D), np.float32)
    for c in range(8):
        b, h = c // 2, c % 2
        outp[b, h * T:(h + 1) * T, :] = res.results[c]["out"]
    return outp
